# version 08 (final)
# baseline (speedup 1.0000x reference)
import math
import numpy as np
import ml_dtypes
import concourse.bass as bass
import concourse.mybir as mybir
from concourse.bass_utils import run_bass_kernel_spmd

F32 = mybir.dt.float32
BF16 = mybir.dt.bfloat16
I32 = mybir.dt.int32
AF = mybir.ActivationFunctionType
ALU = mybir.AluOpType
AX = mybir.AxisListType

HID = 4096
FCN = 32
H = 32
KVH = 8
GQ = H // KVH
D = 128
E0, R0 = 8, 16
E1, K1 = 64, 8
NCORES = 8
EPS_LN = 1e-5

TWO_PI = 2.0 * math.pi
def _split3(x):
    c1 = float(np.float32(int(np.float64(x) * 2048) / 2048.0))
    c2 = float(np.float32(int((np.float64(x) - c1) * 2**24) / 2**24))
    c3 = float(np.float32(np.float64(x) - c1 - c2))
    return c1, c2, c3
PC1, PC2, PC3 = _split3(TWO_PI)
INV_2PI = float(np.float32(1.0 / TWO_PI))
LN1E4 = float(np.float32(math.log(10000.0) / 64.0))

SB_BASE = 16512


class Ctr:
    __slots__ = ("sem", "val")
    def __init__(self, sem):
        self.sem = sem
        self.val = 0


class Ker:
    def __init__(self, nc):
        self.nc = nc
        self.pool = {}
        self.final = {}
        self.waited = {}

    def ctr(self, key):
        if key not in self.pool:
            self.pool[key] = Ctr(self.nc.alloc_semaphore("s_" + "_".join(map(str, key))))
        return self.pool[key]


class Pipe:
    def __init__(self, K, blk, N, nmax=None):
        self.K, self.nc, self.blk, self.N = K, K.nc, blk, N
        self.nmax = nmax if nmax is not None else N
        self.stages = []
        self.eng_ord = {}

    def stage(self, name, eng, emit, deps=(), dma=False, dpi=1, unit=None,
              nsem=None, defer=False, n=None):
        o = self.eng_ord.get(eng, 0)
        self.eng_ord[eng] = o + 1
        kind = "d" if dma else "c"
        if nsem is None:
            nsem = 2 if dma else 1
        ctrs = [self.K.ctr((eng, kind, o, k)) for k in range(nsem)]
        st = dict(name=name, eng=eng, emit=emit, deps=list(deps), dma=dma,
                  dpi=dpi, unit=(unit if unit is not None else (16 if dma else 1)),
                  nsem=nsem, ctrs=ctrs, done=0, base=None, defer=defer,
                  n=(n if n is not None else self.nmax))
        self.stages.append(st)
        return st

    @staticmethod
    def _wait_dep(eng, dep, need):
        if need is None:
            return
        need = min(need, dep["n"])
        if need <= 0:
            return
        j = need - 1
        k = j % dep["nsem"]
        step = dep["unit"] * dep["dpi"]
        eng.wait_ge(dep["ctrs"][k].sem, dep["base"][k] + step * (j // dep["nsem"] + 1))

    def run(self):
        K = self.K
        by_eng = {}
        for s in self.stages:
            by_eng.setdefault(s["eng"], []).append(s)
            s["base"] = [c.val for c in s["ctrs"]]
        for eng_name, stages in by_eng.items():
            def body(eng, stages=stages, eng_name=eng_name):
                w = K.waited.setdefault(eng_name, {})
                for ctr, val in list(K.final.items()):
                    if w.get(ctr, 0) < val:
                        eng.wait_ge(ctr.sem, val)
                        w[ctr] = val
                for i in range(self.N):
                    for st in stages:
                        for dep, fn in st["deps"]:
                            Pipe._wait_dep(eng, dep, fn(i))
                        r = st["emit"](eng, i)
                        if r is None:
                            continue
                        ctr = st["ctrs"][st["done"] % st["nsem"]]
                        if st["dma"]:
                            insts = r if isinstance(r, (list, tuple)) else [r]
                            assert len(insts) == st["dpi"], (st["name"], len(insts))
                            for inst in insts:
                                inst.then_inc(ctr.sem, st["unit"])
                                ctr.val += st["unit"]
                        else:
                            eng.drain().then_inc(ctr.sem, 1)
                            ctr.val += 1
                        st["done"] += 1
            getattr(self.blk, eng_name)(body)
        for s in self.stages:
            if not s["defer"]:
                for c in s["ctrs"]:
                    K.final[c] = c.val


def build_nc(T_C, upto=99):
    NCH = T_C // 128
    S_W = min(512, T_C)
    NS = T_C // S_W
    NP1 = T_C // 128
    NB = 8 if NS == 4 else 2
    AGROW = H * D + 1
    nc = bass.Bass("TRN2", target_bir_lowering=False, debug=False)

    def inp(name, shape, dtype=F32):
        return nc.dram_tensor(name, shape, dtype, kind="ExternalInput").ap()
    x_h = inp("x", [T_C, HID])
    rope_h = inp("ropetab", [128, T_C])
    fmask_h = inp("fmask", [4, 1])
    emask_h = inp("emask", [4, 4])
    iemask_h = inp("iemask", [4, 4])
    wq_h = inp("wq", [H, 128, HID])
    wk_h = inp("wk", [KVH, 128, HID])
    wv_h = inp("wv", [KVH, 128, HID])
    wg_h = inp("wg", [H, 128, HID])
    win_h = inp("win", [128, FCN * 32])
    wo_h = inp("wo", [FCN, 128, HID])
    wbase_h = inp("wbase", [FCN, 128, HID])
    wgate_h = inp("wgate", [128, FCN * E0])
    aflat_h = inp("aflat", [128, FCN * 128])
    bflat_h = inp("bflat", [128, HID])
    w1_h = inp("w1", [FCN, 128, HID])
    maskT_h = inp("maskT", [HID, HID], I32)
    rw1_h = inp("rw1", [128, FCN * E1])
    expb_h = inp("expb", [E1, HID])
    bbase_h = inp("bbase", [128, 32])
    bias1_h = inp("bias1", [128, 32])
    dtb_h = inp("dtb", [32, 1])
    alog_h = inp("alog", [32, 1])
    gnorm_h = inp("gnorm", [1, 128])
    rb1_h = inp("rb1", [E1, 1])
    out_h = nc.dram_tensor("out", [T_C, HID], F32, kind="ExternalOutput").ap()

    def hbm3(w, j):
        return bass.AP(w.tensor, j * 128 * HID, [[HID, 128], [1, HID]])

    ident_h = nc.inline_tensor(np.eye(128, dtype=ml_dtypes.bfloat16), name="c_ident")
    id64_h = nc.inline_tensor(np.eye(64, dtype=np.float32), name="c_id64")
    utri_np = (np.arange(128)[:, None] <= np.arange(128)[None, :]).astype(ml_dtypes.bfloat16)
    utri_h = nc.inline_tensor(utri_np, name="c_utri")
    ones1_h = nc.inline_tensor(np.ones((1, 128), dtype=ml_dtypes.bfloat16), name="c_ones1")
    ones1f_h = nc.inline_tensor(np.ones((1, 128), dtype=np.float32), name="c_ones1f")
    esel_np = np.zeros((E0, 128), dtype=ml_dtypes.bfloat16)
    for e in range(E0):
        esel_np[e, e * R0:(e + 1) * R0] = 1
    esel_h = nc.inline_tensor(esel_np, name="c_esel")
    sel4_np = np.zeros((4, 128), dtype=ml_dtypes.bfloat16)
    for p in range(128):
        sel4_np[p // 32, p] = 1
    sel4_h = nc.inline_tensor(sel4_np, name="c_sel4")
    b32_np = np.zeros((128, 4), dtype=ml_dtypes.bfloat16)
    for r in range(128):
        b32_np[r, r // 32] = 1
    b32_h = nc.inline_tensor(b32_np, name="c_b32")
    iotac_h = nc.inline_tensor(np.arange(64, dtype=np.float32).reshape(64, 1), name="c_iotac")

    def slab(name, shape, dtype):
        return nc.dram_tensor(name, shape, dtype, kind="Internal").ap()
    xn_s = slab("xn_s", [T_C, HID], BF16)
    xhi_s = slab("xhi_s", [T_C, HID], BF16)
    xlo_s = slab("xlo_s", [T_C, HID], BF16)
    gT_s = slab("gT_s", [32, T_C], F32)
    gnd_s = slab("gnd_s", [NCH, 32], F32)
    kd_s = slab("kd_s", [KVH * D, T_C], BF16)
    vd_s = slab("vd_s", [KVH * D, T_C], BF16)
    qd_s = slab("qd_s", [HID, T_C], BF16)
    g_s = slab("g_s", [HID, T_C], BF16)
    xc_s = slab("xc_s", [H, T_C, D], BF16)
    cs_s = slab("cs_s", [H, NCH, D, D], F32)
    yg_s = slab("yg_s", [T_C, HID], BF16)
    f0x_s = slab("f0x_s", [HID, T_C], F32)
    x1hi_s = slab("x1hi_s", [HID, T_C], BF16)
    x1lo_s = slab("x1lo_s", [HID, T_C], BF16)
    xn1_s = slab("xn1_s", [T_C, HID], BF16)
    afhi_s = slab("afhi_s", [HID, T_C], BF16)
    aflo_s = slab("aflo_s", [HID, T_C], BF16)
    rw_s = slab("rw_s", [T_C, 128], BF16)
    wd_s = slab("wd_s", [T_C, 128], BF16)
    wsb_s = slab("wsb_s", [1, T_C], F32)
    wb4_s = slab("wb4_s", [4, 32], F32)
    bmb_s = slab("bmb_s", [128, 128], BF16)
    bm01_s = slab("bm01_s", [128, 128], BF16)
    ag_in = nc.dram_tensor("ag_in", [AGROW, 128], F32, kind="Internal")
    ag_out = nc.dram_tensor("ag_out", [4 * AGROW, 128], F32, kind="Internal")

    cur = [SB_BASE]
    def at(name, shape, dtype, off=None):
        if off is None:
            off = cur[0]
            nb = int(np.prod(shape[1:])) * mybir.dt.size(dtype)
            cur[0] = (off + nb + 31) & ~31
        return nc.alloc_sbuf_tensor_at(name, shape, dtype, offset=off).ap()

    ident = at("ident", [128, 128], BF16)
    id64 = at("id64", [64, 64], F32)
    utri = at("utri", [128, 128], BF16)
    ones1 = at("ones1", [1, 128], BF16)
    ones1f = at("ones1f", [1, 128], F32)
    esel = at("esel", [E0, 128], BF16)
    sel4 = at("sel4", [4, 128], BF16)
    b32 = at("b32", [128, 4], BF16)
    iotac = at("iotac", [64, 1], F32)
    invfc = at("invfc", [64, 1], F32)
    gnormr = at("gnormr", [1, 128], F32)
    gnorm_bc = at("gnorm_bc", [128, 128], F32)
    dtb = at("dtb_c", [32, 1], F32)
    alog = at("alog_c", [32, 1], F32)
    acol = at("acol", [32, 1], F32)
    bbase = at("bbase_c", [128, 32], F32)
    bias1c = at("bias1c", [128, 32], F32)
    rb1 = at("rb1_c", [E1, 1], F32)
    fmaskc = at("fmaskc", [4, 1], F32)
    emaskc = at("emaskc", [4, 4], F32)
    iemaskc = at("iemaskc", [4, 4], F32)
    epsc = at("epsc", [128, 1], F32)
    zer32 = at("zer32", [32, 128], F32)
    zer64 = at("zer64", [128, 64], F32)
    csd = at("csd", [128, T_C], BF16)
    G_T = at("G_T", [32, T_C], F32)
    dtT_off = cur[0]
    dt_T = at("dt_T", [32, T_C], F32)
    cur[0] = max(cur[0], dtT_off + 8192)
    expb_sb = at("expb_sb", [E1, HID], BF16, off=dtT_off)
    pos_sl = at("pos_sl", [1, S_W], F32, off=dtT_off)
    dt_tok = at("dt_tok", [128, NCH * 32], F32)
    G_tok = at("G_tok", [128, NCH * 32], F32)
    eG_tok = at("eG_tok", [128, NCH * 32], F32)
    eGd_tok = at("eGd_tok", [128, NCH * 32], F32)
    dec_bc = at("dec_bc", [128, NCH * 32], F32)
    Gend = at("Gend", [32, NCH], F32)
    Gsum = at("Gsum", [32, 1], F32)
    pcol = at("pcol", [32, 1], F32)
    t2_off = cur[0]
    t2 = at("t2", [128, 512], F32)
    t3 = at("t3", [128, 512], F32)
    PERS_END = cur[0]
    assert PERS_END <= SB_BASE + 40800, PERS_END

    B_OFF = SB_BASE + 40800
    C_OFF = B_OFF + 24576
    A_OFF = C_OFF + 16384
    assert A_OFF + 131072 <= 229376 - 32

    wbf = [at(f"wbf{i}", [128, HID], BF16, off=B_OFF + i * 8192) for i in range(2)]
    state = at("state", [128, H * 128], F32, off=B_OFF)
    mbf = [at(f"mbf{i}", [128, HID], BF16, off=B_OFF + i * 8192) for i in range(2)]
    xhiT_t = [at(f"xhiT{i}", [128, S_W], BF16, off=B_OFF + 16384 + i * 1024) for i in range(2)]
    xloT_t = [at(f"xloT{i}", [128, S_W], BF16, off=B_OFF + 18432 + i * 1024) for i in range(2)]
    rwT = at("rwT", [128, T_C], BF16, off=B_OFF + 20480)
    rw1bf = at("rw1bf", [128, FCN * E1], BF16, off=B_OFF)
    l1T = at("l1T", [E1, T_C], F32, off=B_OFF + 8192)
    wdT = at("wdT", [128, T_C], BF16, off=B_OFF + 16384)
    wsum_bc = at("wsum_bc", [128, T_C], BF16, off=B_OFF + 20480)

    kq2 = [at(f"kq2{i}", [128, S_W], BF16, off=C_OFF + i * 1024) for i in range(2)]
    l0T = at("l0T", [E0, T_C], F32, off=C_OFF + 2048)
    wgbf = at("wgbf", [128, FCN * E0], BF16, off=C_OFF + 10240)
    l0tok = at("l0tok", [128, NCH * E0], F32, off=C_OFF + 10752)
    r8 = [at(f"r8_{i}", [128, 8], F32, off=C_OFF + 11264 + i * 32) for i in range(6)]
    rcol = at("rcol", [128, 4], F32, off=C_OFF + 11456)
    rw8 = [at(f"rw8_{i}", [128, 128], BF16, off=C_OFF + 11520 + i * 256) for i in range(2)]
    ktok = [at(f"ktok{i}", [128, 128], BF16, off=C_OFF + i * 256) for i in range(2)]
    vtok = [at(f"vtok{i}", [128, 128], BF16, off=C_OFF + 512 + i * 256) for i in range(2)]
    xct = [at(f"xct{i}", [128, 128], BF16, off=C_OFF + 1024 + i * 256) for i in range(8)]
    kdt = [at(f"kdt{i}", [128, 128], BF16, off=C_OFF + 3072 + i * 256) for i in range(8)]
    cst = [at(f"cst{i}", [128, 128], F32, off=C_OFF + 5120 + i * 512) for i in range(8)]
    bflat_sb = at("bflat_sb", [128, HID], BF16, off=C_OFF)
    hwT = at("hwT", [128, T_C], BF16, off=C_OFF + 8192)
    rwx = at("rwx", [128, T_C], BF16, off=C_OFF + 12288)
    pmrow = at("pmrow", [1, 128], F32, off=C_OFF)
    wrow = at("wrow", [1, 128], F32, off=C_OFF + 512)
    w4 = at("w4", [4, 32], F32, off=C_OFF + 1024)
    cfac = at("cfac", [4, 32], F32, off=C_OFF + 1152)
    wbc_sb = at("wbc_sb", [128, 128], F32, off=C_OFF + 1280)
    spt = [at(f"spt{i}", [128, 128], F32, off=C_OFF + 2048 + i * 512) for i in range(8)]
    f0t = [at(f"f0t{i}", [128, S_W], F32, off=C_OFF + 12288 + i * 2048) for i in range(2)]
    x1t = [at(f"x1t{i}", [128, S_W], F32, off=C_OFF + 4096 + i * 2048) for i in range(2)]
    x1hi_t = [at(f"x1hi{i}", [128, S_W], BF16, off=C_OFF + 8192 + i * 1024) for i in range(2)]
    x1lo_t = [at(f"x1lo{i}", [128, S_W], BF16, off=C_OFF + 10240 + i * 1024) for i in range(2)]
    s1sb = at("s1sb", [4, HID], BF16, off=C_OFF)
    mr1 = at("mr1", [4, 2048], BF16, off=C_OFF + 8192)
    mr2 = at("mr2", [4, 1024], BF16, off=C_OFF + 12288)
    mr3 = at("mr3", [4, 512], BF16, off=C_OFF + 14336)
    mr4 = at("mr4", [4, 256], BF16, off=C_OFF + 15360)
    mr5 = [at(f"mr5_{i}", [4, 128], BF16, off=C_OFF + 15872 + i * 256) for i in range(2)]
    bmld = at("bmld", [128, 128], BF16, off=C_OFF + 8192)
    bm01t = at("bm01t", [128, 128], BF16, off=C_OFF + 8448)
    bmr = [at(f"bmr{i}", [4, 128], BF16, off=C_OFF + 8704 + i * 256) for i in range(2)]
    mexp = at("mexp", [128, FCN * 128], BF16, off=C_OFF)
    l1tok = at("l1tok", [128, E1], F32, off=C_OFF + 8192)
    etok = at("etok", [128, E1], F32, off=C_OFF + 8448)
    ew = at("ew", [128, E1], F32, off=C_OFF + 8704)
    sel8 = at("sel8", [128, E1], F32, off=C_OFF + 8960)
    wdtok = at("wdtok", [128, E1], F32, off=C_OFF + 9216)
    cc1 = at("cc1", [128, E1], F32, off=C_OFF + 9472)
    wdbf = [at(f"wdbf{i}", [128, 128], BF16, off=C_OFF + 10240 + i * 256) for i in range(2)]
    r64 = at("r64", [128, 8], F32, off=C_OFF + 9984)
    wsc = [at(f"wsc{i}", [128, 1], F32, off=C_OFF + 10752 + i * 32) for i in range(2)]
    a1t0 = [at(f"a1t0_{i}", [128, S_W], F32, off=C_OFF + 8192 + i * 2048) for i in range(2)]
    a1t1 = at("a1t1", [128, S_W], F32, off=C_OFF + 12288)
    afhi_t = [at(f"afhi{i}", [128, S_W], BF16, off=C_OFF + 14336 + i * 1024) for i in range(2)]
    aflo_t = [at(f"aflo{i}", [128, S_W], BF16, off=t2_off + i * S_W * 2) for i in range(2)]

    xn_T = at("xn_T", [128, FCN * T_C], BF16, off=A_OFF)
    ygT = at("ygT", [128, FCN * T_C], BF16, off=A_OFF)
    xn1T = at("xn1T", [128, FCN * T_C], BF16, off=A_OFF)
    p1xt = [at(f"p1x{i}", [128, HID], F32, off=A_OFF + i * 16384) for i in range(2)]
    p1ot = [at(f"p1o{i}", [128, 3 * HID], BF16, off=A_OFF + 32768 + i * 24576) for i in range(2)]
    p1sq = at("p1sq", [128, HID], BF16, off=A_OFF + 81920)
    qdt = [at(f"qdt{i}", [128, 128], BF16, off=A_OFF + i * 256) for i in range(8)]
    kdt5 = [at(f"kdt5{i}", [128, 128], BF16, off=A_OFF + 2048 + i * 256) for i in range(2)]
    xct5 = [at(f"xct5{i}", [128, 128], BF16, off=A_OFF + 2560 + i * 256) for i in range(8)]
    cst5 = [at(f"cst5{i}", [128, 128], F32, off=A_OFF + 4608 + i * 512) for i in range(8)]
    gt5 = [at(f"gt5{i}", [128, 128], BF16, off=A_OFF + 8704 + i * 256) for i in range(8)]
    grow5 = [at(f"grow5{i}", [1, 4 * 128], F32, off=A_OFF + 10752 + i * 2048) for i in range(2)]
    mm5 = [at(f"mm5{i}", [128, 128], F32, off=A_OFF + 14848 + i * 512) for i in range(8)]
    dcy5 = [at(f"dcy5{i}", [128, 128], F32, off=A_OFF + 18944 + i * 512) for i in range(8)]
    sd5 = [at(f"sd5{i}", [128, 128], BF16, off=A_OFF + 23040 + i * 256) for i in range(8)]
    yt5 = [at(f"yt5{i}", [128, 128], F32, off=A_OFF + 25088 + i * 512) for i in range(8)]
    sbf5 = [at(f"sbf5{i}", [128, 128], BF16, off=A_OFF + 29184 + i * 256) for i in range(8)]
    gsil5 = [at(f"gsil5{i}", [128, 128], F32, off=A_OFF + 31232 + i * 512) for i in range(8)]
    sq5 = at("sq5", [128, 128], F32, off=A_OFF + 35328)
    ivr5 = [at(f"ivr5{i}", [128, 1], F32, off=A_OFF + 35840 + i * 32) for i in range(8)]
    ygs5 = [at(f"ygs5{i}", [128, 128], BF16, off=A_OFF + 36352 + i * 256) for i in range(8)]
    p5hi = [at(f"p5hi{i}", [128, HID], BF16, off=A_OFF + i * 8192) for i in range(2)]
    p5lo = [at(f"p5lo{i}", [128, HID], BF16, off=A_OFF + 16384 + i * 8192) for i in range(2)]
    p5x = [at(f"p5x{i}", [128, HID], F32, off=A_OFF + 32768 + i * 16384) for i in range(2)]
    p5o = [at(f"p5o{i}", [128, HID], BF16, off=A_OFF + 65536 + i * 8192) for i in range(2)]
    p5sq = at("p5sq", [128, HID], BF16, off=A_OFF + 81920)
    p8a = [at(f"p8a{i}", [128, HID], BF16, off=A_OFF + i * 8192) for i in range(4)]
    p8b = [at(f"p8b{i}", [128, HID], BF16, off=A_OFF + 32768 + i * 8192) for i in range(4)]
    p8o = [at(f"p8o{i}", [128, HID], F32, off=A_OFF + 65536 + i * 16384) for i in range(2)]
    p8t = [at(f"p8t{i}", [128, HID], F32, off=A_OFF + 98304 + i * 16384) for i in range(2)]

    PS = [nc.alloc_psum_tensor(f"PS{i}", [128, 512], F32).ap() for i in range(8)]

    K = Ker(nc)

    with nc.Block() as blk:
        pp = Pipe(K, blk, 1)
        def ld_consts(eng, i):
            r = []
            for dst, src in [(ident, ident_h.ap()), (id64, id64_h.ap()),
                             (utri, utri_h.ap()), (ones1, ones1_h.ap()),
                             (ones1f, ones1f_h.ap()), (esel, esel_h.ap()),
                             (sel4, sel4_h.ap()), (b32, b32_h.ap()),
                             (dtb, dtb_h), (alog, alog_h), (gnormr, gnorm_h),
                             (bbase, bbase_h), (bias1c, bias1_h), (rb1, rb1_h),
                             (fmaskc, fmask_h), (emaskc, emask_h), (iemaskc, iemask_h)]:
                r.append(eng.dma_start(out=dst, in_=src))
            return r
        st_ld = pp.stage("c_ld", "sync", ld_consts, dma=True, dpi=17)

        def mk_const(eng, i):
            eng.memset(zer32, 0.0)
            eng.memset(zer64, 0.0)
            eng.memset(epsc, EPS_LN)
            eng.tensor_scalar(acol, alog, 1.0, None, ALU.mult)
            return True
        st_z = pp.stage("c_z", "vector", mk_const, deps=[(st_ld, lambda i: 1)])

        def mk_act(eng, i):
            eng.activation(acol, acol, AF.Exp)
            return True
        st_e = pp.stage("c_e", "scalar", mk_act, deps=[(st_z, lambda i: 1)])

        def mk_bc(eng, i):
            eng.matmul(PS[0][:, 0:128], ones1f, gnormr, start=True, stop=True)
            return True
        st_bc = pp.stage("c_bc", "tensor", mk_bc, deps=[(st_e, lambda i: 1)])

        def mk_fin(eng, i):
            eng.tensor_copy(gnorm_bc, PS[0][:, 0:128])
            eng.tensor_scalar(acol, acol, -1.0, None, ALU.mult)
            return True
        st_f = pp.stage("c_f", "vector", mk_fin, deps=[(st_bc, lambda i: 1)])
        pp.run()

        rp = Pipe(K, blk, 1)
        rp.stage("c_rt", "gpsimd",
                 lambda eng, i: [eng.dma_start(out=csd, in_=rope_h)], dma=True)
        rp.run()

    if upto < 1:
        return nc

    with nc.Block() as blk:
        pp = Pipe(K, blk, NP1)
        st_ld = pp.stage("p1_ld", "sync",
                         lambda eng, i: [eng.dma_start(out=p1xt[i % 2], in_=x_h[i * 128:(i + 1) * 128, :])],
                         dma=True)
        def p1_sq(eng, i):
            eng.activation(p1sq, p1xt[i % 2], AF.Square, accum_out=t2[:, 0:1])
            return True
        st_sq = pp.stage("p1_sq", "scalar", p1_sq, deps=[(st_ld, lambda i: i + 1)])
        def p1_stats(eng, i):
            eng.reduce_sum(t2[:, 1:2], p1xt[i % 2], axis=AX.X)
            eng.drain()
            eng.tensor_scalar(t2[:, 1:2], t2[:, 1:2], 1.0 / HID, None, ALU.mult)
            eng.tensor_scalar(t2[:, 0:1], t2[:, 0:1], 1.0 / HID, None, ALU.mult)
            eng.drain()
            eng.scalar_tensor_tensor(t2[:, 2:3], t2[:, 1:2], 1.0, t2[:, 1:2], ALU.mult, ALU.mult)
            eng.drain()
            eng.tensor_tensor(t2[:, 0:1], t2[:, 0:1], t2[:, 2:3], ALU.subtract)
            return True
        st_v1 = pp.stage("p1_v1", "vector", p1_stats, deps=[(st_sq, lambda i: i + 1)])
        def p1_r(eng, i):
            eng.activation(t2[:, 3:4], t2[:, 0:1], AF.Ln, bias=epsc[:, 0:1])
            eng.drain()
            eng.activation(t2[:, 3:4], t2[:, 3:4], AF.Exp, scale=-0.5)
            return True
        st_r = pp.stage("p1_r", "scalar", p1_r, deps=[(st_v1, lambda i: i + 1)])
        def p1_apply(eng, i):
            o = p1ot[i % 2]
            eng.tensor_scalar(o[:, 0:HID], p1xt[i % 2], t2[:, 1:2], t2[:, 3:4],
                              ALU.subtract, ALU.mult)
            eng.tensor_copy(o[:, HID:2 * HID], p1xt[i % 2])
            eng.drain()
            eng.tensor_tensor(o[:, 2 * HID:3 * HID], p1xt[i % 2], o[:, HID:2 * HID], ALU.subtract)
            return True
        st_ap = pp.stage("p1_ap", "vector", p1_apply, deps=[(st_r, lambda i: i + 1)])
        def p1_st(eng, i):
            r = slice(i * 128, (i + 1) * 128)
            o = p1ot[i % 2]
            return [eng.dma_start(out=xn_s[r, :], in_=o[:, 0:HID]),
                    eng.dma_start(out=xhi_s[r, :], in_=o[:, HID:2 * HID]),
                    eng.dma_start(out=xlo_s[r, :], in_=o[:, 2 * HID:3 * HID])]
        st_st = pp.stage("p1_st", "gpsimd", p1_st, deps=[(st_ap, lambda i: i + 1)],
                         dma=True, dpi=3)
        st_ld["deps"].append((st_ap, lambda i: i - 1))
        st_ap["deps"].append((st_st, lambda i: i - 1))
        st_sq["deps"].append((st_r, lambda i: i))
        st_v1["deps"].append((st_ap, lambda i: i))
        st_v1["deps"].append((st_r, lambda i: i))
        pp.run()

    if upto < 2:
        return nc

    with nc.Block() as blk:
        pp = Pipe(K, blk, 1)
        def ld_xnT(eng, i):
            return [eng.dma_start_transpose(xn_T[:, fc * T_C:(fc + 1) * T_C],
                                            xn_s[:, fc * 128:(fc + 1) * 128])
                    for fc in range(FCN)]
        st_xnT = pp.stage("pp_xnT", "sync", ld_xnT, dma=True, dpi=FCN)
        st_wdt = pp.stage("pp_wdt", "gpsimd",
                          lambda eng, i: [eng.dma_start(out=winbf, in_=win_h)],
                          dma=True)
        if upto < 1.4:
            pp.run()
            return nc
        winbf = at("winbf", [128, FCN * 32], BF16, off=C_OFF + 10240)
        def dt_mm(eng, i):
            for s in range(NS):
                for fc in range(FCN):
                    eng.matmul(PS[s][0:32, 0:S_W], winbf[:, fc * 32:(fc + 1) * 32],
                               xn_T[:, fc * T_C + s * S_W: fc * T_C + (s + 1) * S_W],
                               start=(fc == 0), stop=(fc == FCN - 1))
            return True
        st_mm = pp.stage("pp_dtmm", "tensor", dt_mm,
                         deps=[(st_xnT, lambda i: 1), (st_wdt, lambda i: 1)])
        def dt_act(eng, i):
            for s in range(NS):
                sl = slice(s * S_W, (s + 1) * S_W)
                eng.activation(t3[0:32, 0:S_W], PS[s][0:32, 0:S_W], AF.Exp, bias=dtb[:, 0:1])
                eng.drain()
                eng.activation(dt_T[:, sl], t3[0:32, 0:S_W], AF.Ln, bias=1.0)
                eng.drain()
            return True
        st_da = pp.stage("pp_dta", "scalar", dt_act, deps=[(st_mm, lambda i: 1)])
        if upto < 1.6:
            pp.run()
            return nc
        def g_chain(eng, i):
            for n in range(NCH):
                sl = slice(n * 128, (n + 1) * 128)
                eng.tensor_tensor_scan(G_T[:, sl], dt_T[:, sl], zer32, 0.0, ALU.add, ALU.add)
            eng.drain()
            eng.tensor_scalar(G_T, G_T, acol, None, ALU.mult)
            eng.drain()
            eng.tensor_copy(Gend, bass.AP(G_T.tensor, 127, [[T_C, 32], [128, NCH]]))
            eng.drain()
            eng.reduce_sum(Gsum, Gend, axis=AX.X)
            return True
        st_g = pp.stage("pp_g", "vector", g_chain, deps=[(st_da, lambda i: 1)])
        def g_tp(eng, i):
            for n in range(NCH):
                eng.transpose(PS[4][:, n * 32:(n + 1) * 32],
                              dt_T[:, n * 128:(n + 1) * 128], id64[0:32, 0:32])
                eng.transpose(PS[5][:, n * 32:(n + 1) * 32],
                              G_T[:, n * 128:(n + 1) * 128], id64[0:32, 0:32])
            eng.transpose(PS[6][0:NCH, 0:32], Gend, id64[0:32, 0:32])
            return True
        st_tp = pp.stage("pp_tp", "tensor", g_tp, deps=[(st_g, lambda i: 1)])
        def tok_cp(eng, i):
            eng.copy(dt_tok, PS[4][:, 0:NCH * 32])
            eng.copy(G_tok, PS[5][:, 0:NCH * 32])
            eng.copy(t2[0:NCH, 0:32], PS[6][0:NCH, 0:32])
            eng.drain()
            eng.activation(eG_tok, G_tok, AF.Exp)
            eng.drain()
            eng.activation(pcol, Gsum, AF.Exp)
            return True
        st_tc = pp.stage("pp_tc", "scalar", tok_cp, deps=[(st_tp, lambda i: 1)])
        if upto < 1.8:
            pp.run()
            return nc
        def bnc_st(eng, i):
            return [eng.dma_start(out=gnd_s, in_=t2[0:NCH, 0:32]),
                    eng.dma_start(out=gT_s, in_=G_T),
                    eng.dma_start(out=ag_in.ap()[4096:4097, 0:32], in_=pcol),
                    eng.dma_start(out=ag_in.ap()[4096:4097, 32:128],
                                  in_=zer32[0:1, 0:96])]
        st_bn = pp.stage("pp_bn", "gpsimd", bnc_st, dma=True, dpi=4,
                         deps=[(st_tc, lambda i: 1)])
        def ld_gnd(eng, i):
            return [eng.dma_start(out=t3[0:1, 0:NCH * 32],
                                  in_=bass.AP(gnd_s.tensor, 0, [[NCH * 32, 1], [1, NCH * 32]]))]
        st_lg = pp.stage("pp_lg", "sync", ld_gnd, dma=True, deps=[(st_bn, lambda i: 1)])
        if upto < 1.9:
            pp.run()
            return nc
        def gbc_mm(eng, i):
            eng.matmul(PS[7][:, 0:NCH * 32], ones1f, t3[0:1, 0:NCH * 32], start=True, stop=True)
            return True
        st_gb = pp.stage("pp_gb", "tensor", gbc_mm, deps=[(st_lg, lambda i: 1)])
        def dec_a(eng, i):
            eng.copy(t3[:, 0:NCH * 32], PS[7][:, 0:NCH * 32])
            eng.activation(dec_bc, PS[7][:, 0:NCH * 32], AF.Exp)
            return True
        st_dc = pp.stage("pp_dc", "scalar", dec_a, deps=[(st_gb, lambda i: 1)])
        if upto < 1.95:
            pp.run()
            return nc
        def egd_v(eng, i):
            eng.tensor_tensor(t3[:, 0:NCH * 32], t3[:, 0:NCH * 32], G_tok, ALU.subtract)
            return True
        st_ev = pp.stage("pp_ev", "vector", egd_v,
                         deps=[(st_dc, lambda i: 1), (st_tc, lambda i: 1)])
        def egd_e(eng, i):
            eng.activation(eGd_tok, t3[:, 0:NCH * 32], AF.Exp)
            return True
        pp.stage("pp_ee", "scalar", egd_e, deps=[(st_ev, lambda i: 1), (st_dc, lambda i: 1)])
        pp.run()

    if upto < 3:
        return nc

    def proj_pass(name, n_oc, w_src, evac_eng, evac_emit, store_emit,
                  store_dpi=1, extra_mm=None, extra_ld=None, extra_ld_dpi=1,
                  rhs=None, wtile=None):
        rhs = rhs if rhs is not None else xn_T
        wt = wtile if wtile is not None else wbf
        NI = n_oc * NS
        with nc.Block() as blk:
            pp = Pipe(K, blk, NI + 1, nmax=NI)
            def w_ld(eng, i):
                if i >= NI or i % NS != 0:
                    return None
                oc = i // NS
                return [eng.dma_start(out=wt[oc % 2], in_=w_src(oc))]
            st_w = pp.stage(name + "_w", "gpsimd", w_ld, dma=True, n=n_oc)
            st_x = None
            if extra_ld is not None:
                def x_ld(eng, i):
                    if i >= NI:
                        return None
                    return extra_ld(eng, i)
                st_x = pp.stage(name + "_x", "sync", x_ld, dma=True, dpi=extra_ld_dpi)
            def mm_emit(eng, i):
                if i >= NI:
                    return None
                oc, s = divmod(i, NS)
                wb = wt[oc % 2]
                P = PS[i % NB]
                for fc in range(FCN):
                    eng.matmul(P[:, 0:S_W], wb[:, fc * 128:(fc + 1) * 128],
                               rhs[:, fc * T_C + s * S_W: fc * T_C + (s + 1) * S_W],
                               start=(fc == 0), stop=(extra_mm is None and fc == FCN - 1))
                if extra_mm is not None:
                    extra_mm(eng, i, P)
                return True
            st_mm = pp.stage(name + "_mm", "tensor", mm_emit,
                             deps=[(st_w, lambda i: i // NS + 1)])
            def ev(eng, i):
                if i >= NI:
                    return None
                return evac_emit(eng, i, PS[i % NB])
            st_ev = pp.stage(name + "_ev", evac_eng, ev, deps=[(st_mm, lambda i: i + 1)])
            st_mm["deps"].append((st_ev, lambda i: i - NB + 1))
            st_w["deps"].append((st_mm, lambda i: (i // NS - 1) * NS))
            if st_x is not None:
                st_ev["deps"].append((st_x, lambda i: i + 1))
            if store_emit is not None:
                def st_emit(eng, i):
                    if i == 0:
                        return None
                    return store_emit(eng, i - 1)
                st_st = pp.stage(name + "_st", "gpsimd", st_emit,
                                 deps=[(st_ev, lambda i: i)], dma=True, dpi=store_dpi)
                st_ev["deps"].append((st_st, lambda i: i - 1))
            pp.run()

    def rope_evac(eng, i, P):
        oc, s = divmod(i, NS)
        sl = slice(s * S_W, (s + 1) * S_W)
        o = kq2[i % 2]
        eng.tensor_tensor(t2[0:64, 0:S_W], P[0:64, 0:S_W], csd[0:64, sl], ALU.mult)
        eng.tensor_tensor(t2[64:128, 0:S_W], P[64:128, 0:S_W], csd[0:64, sl], ALU.mult)
        eng.tensor_tensor(t3[0:64, 0:S_W], P[64:128, 0:S_W], csd[64:128, sl], ALU.mult)
        eng.tensor_tensor(t3[64:128, 0:S_W], P[0:64, 0:S_W], csd[64:128, sl], ALU.mult)
        eng.drain()
        eng.tensor_tensor(o[0:64, 0:S_W], t2[0:64, 0:S_W], t3[0:64, 0:S_W], ALU.subtract)
        eng.tensor_tensor(o[64:128, 0:S_W], t2[64:128, 0:S_W], t3[64:128, 0:S_W], ALU.add)
        return True

    def cp_evac(eng, i, P):
        eng.activation(kq2[i % 2], P[:, 0:S_W], AF.Copy)
        return True

    def mk_store(dst):
        def f(eng, j):
            oc, s = divmod(j, NS)
            return [eng.dma_start(
                out=dst[oc * 128:(oc + 1) * 128, s * S_W:(s + 1) * S_W],
                in_=kq2[j % 2])]
        return f

    proj_pass("k", KVH, lambda oc: hbm3(wk_h, oc), "vector", rope_evac, mk_store(kd_s))
    proj_pass("v", KVH, lambda oc: hbm3(wv_h, oc), "scalar", cp_evac, mk_store(vd_s))

    if upto < 4:
        return nc

    with nc.Block() as blk:
        NIT = KVH * NCH
        pp = Pipe(K, blk, NIT + 1, nmax=NIT)
        def b_z(eng, i):
            if i > 0:
                return None
            eng.memset(state, 0.0)
            return True
        st_z = pp.stage("b_z", "gpsimd", b_z, n=1)
        def b_ld(eng, i):
            if i >= NIT:
                return None
            kv, n = divmod(i, NCH)
            src_k = bass.AP(kd_s.tensor, (kv * 128) * T_C + n * 128, [[T_C, 128], [1, 128]])
            src_v = bass.AP(vd_s.tensor, (kv * 128) * T_C + n * 128, [[T_C, 128], [1, 128]])
            return [eng.dma_start_transpose(ktok[i % 2], src_k),
                    eng.dma_start_transpose(vtok[i % 2], src_v)]
        st_ld = pp.stage("b_ld", "sync", b_ld, dma=True, dpi=2)
        def b_xc(eng, i):
            if i >= NIT:
                return None
            kv, n = divmod(i, NCH)
            for h4 in range(GQ):
                c = n * 32 + kv * GQ + h4
                eng.tensor_scalar(xct[(i % 2) * 4 + h4], vtok[i % 2],
                                  dt_tok[:, c:c + 1], None, ALU.mult)
                eng.tensor_scalar(kdt[(i % 2) * 4 + h4], ktok[i % 2],
                                  eGd_tok[:, c:c + 1], None, ALU.mult)
            return True
        st_xc = pp.stage("b_xc", "vector", b_xc, deps=[(st_ld, lambda i: i + 1)])
        def b_mm(eng, i):
            if i >= NIT:
                return None
            for h4 in range(GQ):
                eng.matmul(PS[(i % 2) * 4 + h4][:, 0:128], kdt[(i % 2) * 4 + h4],
                           xct[(i % 2) * 4 + h4], start=True, stop=True)
            return True
        st_mm = pp.stage("b_mm", "tensor", b_mm, deps=[(st_xc, lambda i: i + 1)])
        def b_csc(eng, i):
            if i >= NIT:
                return None
            for h4 in range(GQ):
                eng.copy(cst[(i % 2) * 4 + h4], PS[(i % 2) * 4 + h4][:, 0:128])
            return True
        st_cs = pp.stage("b_cs", "scalar", b_csc, deps=[(st_mm, lambda i: i + 1)])
        def b_scan(eng, i):
            if i >= NIT:
                return None
            kv, n = divmod(i, NCH)
            for h4 in range(GQ):
                h = kv * GQ + h4
                c = n * 32 + h
                hs = slice(h * 128, (h + 1) * 128)
                eng.tensor_scalar(state[:, hs], state[:, hs], dec_bc[:, c:c + 1], None, ALU.mult)
                eng.drain()
                eng.tensor_tensor(state[:, hs], state[:, hs], cst[(i % 2) * 4 + h4], ALU.add)
                eng.drain()
            return True
        st_sc = pp.stage("b_scan", "gpsimd", b_scan,
                         deps=[(st_cs, lambda i: i + 1), (st_z, lambda i: 1)])
        def b_st(eng, i):
            if i == 0:
                return None
            j = i - 1
            kv, n = divmod(j, NCH)
            r = []
            for h4 in range(GQ):
                h = kv * GQ + h4
                r.append(eng.dma_start(
                    out=bass.AP(xc_s.tensor, (h * T_C + n * 128) * D, [[D, 128], [1, D]]),
                    in_=xct[(j % 2) * 4 + h4]))
                r.append(eng.dma_start(
                    out=bass.AP(cs_s.tensor, (h * NCH + n) * D * D, [[D, 128], [1, D]]),
                    in_=cst[(j % 2) * 4 + h4]))
            return r
        st_st = pp.stage("b_st", "gpsimd", b_st, dma=True, dpi=2 * GQ,
                         deps=[(st_cs, lambda i: i), (st_xc, lambda i: i)])
        st_xc["deps"].append((st_st, lambda i: i - 1))
        st_cs["deps"].append((st_st, lambda i: i - 1))
        st_cs["deps"].append((st_sc, lambda i: i - 1))
        st_mm["deps"].append((st_sc, lambda i: i - 1))
        st_mm["deps"].append((st_cs, lambda i: i - 1))
        st_ld["deps"].append((st_xc, lambda i: i - 1))
        pp.run()

    with nc.Block() as blk:
        pp = Pipe(K, blk, 1)
        def s_st(eng, i):
            return [eng.dma_start(out=ag_in.ap()[h * 128:(h + 1) * 128, :],
                                  in_=state[:, h * 128:(h + 1) * 128])
                    for h in range(H)]
        st_s = pp.stage("ag_st", "gpsimd", s_st, dma=True, dpi=H)
        def coll(eng, i):
            return [eng.collective_compute(
                "AllGather", ALU.bypass,
                replica_groups=[[0, 1, 2, 3], [4, 5, 6, 7]],
                ins=[ag_in.ap().opt()],
                outs=[ag_out.ap().opt()])]
        st_cc = pp.stage("ag_cc", "gpsimd", coll, deps=[(st_s, lambda i: 1)],
                         dma=True, unit=1, defer=True)
        st_cc["ctrs"] = [K.ctr(("cc", "x", 0, 0))]
        st_cc["nsem"] = 1
        pp.run()
    cc_ctr = st_cc["ctrs"][0]

    if upto < 5:
        return nc

    with nc.Block() as blk:
        pp = Pipe(K, blk, 1)
        st_wg = pp.stage("c2_wg", "gpsimd",
                         lambda eng, i: [eng.dma_start(out=wgbf, in_=wgate_h)], dma=True)
        def l0_mm(eng, i):
            for s in range(NS):
                for fc in range(FCN):
                    eng.matmul(PS[s][0:E0, 0:S_W], wgbf[:, fc * E0:(fc + 1) * E0],
                               xn_T[:, fc * T_C + s * S_W: fc * T_C + (s + 1) * S_W],
                               start=(fc == 0), stop=(fc == FCN - 1))
            return True
        st_lm = pp.stage("c2_lmm", "tensor", l0_mm, deps=[(st_wg, lambda i: 1)])
        def l0_cp(eng, i):
            for s in range(NS):
                eng.copy(l0T[:, s * S_W:(s + 1) * S_W], PS[s][0:E0, 0:S_W])
            return True
        st_lc = pp.stage("c2_lcp", "scalar", l0_cp, deps=[(st_lm, lambda i: 1)])
        def l0_tp(eng, i):
            for n in range(NCH):
                eng.transpose(PS[4][:, n * E0:(n + 1) * E0],
                              l0T[:, n * 128:(n + 1) * 128], id64[0:E0, 0:E0])
            return True
        st_lt = pp.stage("c2_ltp", "tensor", l0_tp, deps=[(st_lc, lambda i: 1)])
        def l0_tok(eng, i):
            eng.tensor_copy(l0tok, PS[4][:, 0:NCH * E0])
            return True
        st_tk = pp.stage("c2_tok", "vector", l0_tok, deps=[(st_lt, lambda i: 1)])
        pp.run()

        t2p = Pipe(K, blk, NCH)
        def t_pre(eng, n):
            ln = l0tok[:, n * E0:(n + 1) * E0]
            eng.reduce_max(rcol[:, 0:1], ln, axis=AX.X)
            eng.drain()
            eng.tensor_scalar(rcol[:, 1:2], rcol[:, 0:1], -1.0, None, ALU.mult)
            return True
        s_pre = t2p.stage("t2_pre", "vector", t_pre)
        def t_exp(eng, n):
            eng.activation(r8[0], l0tok[:, n * E0:(n + 1) * E0], AF.Exp, bias=rcol[:, 1:2])
            return True
        s_exp = t2p.stage("t2_exp", "scalar", t_exp, deps=[(s_pre, lambda n: n + 1)])
        def t_sel(eng, n):
            eng.tensor_scalar(r8[1], r8[0], 1.0, None, ALU.is_ge)
            eng.drain()
            eng.tensor_tensor_scan(r8[2], r8[1], zer64[:, 0:8], 0.0, ALU.add, ALU.add)
            eng.drain()
            eng.tensor_scalar(r8[2], r8[2], 1.0, None, ALU.is_equal)
            eng.drain()
            eng.tensor_tensor(r8[1], r8[1], r8[2], ALU.mult)
            eng.drain()
            eng.scalar_tensor_tensor(r8[3], r8[1], -2.0, r8[0], ALU.mult, ALU.add)
            eng.drain()
            eng.reduce_max(rcol[:, 2:3], r8[3], axis=AX.X)
            eng.drain()
            eng.tensor_scalar(r8[4], r8[3], rcol[:, 2:3], None, ALU.is_ge)
            eng.drain()
            eng.tensor_tensor_scan(r8[5], r8[4], zer64[:, 0:8], 0.0, ALU.add, ALU.add)
            eng.drain()
            eng.tensor_scalar(r8[5], r8[5], 1.0, None, ALU.is_equal)
            eng.drain()
            eng.tensor_tensor(r8[4], r8[4], r8[5], ALU.mult)
            eng.drain()
            eng.tensor_tensor(r8[1], r8[1], r8[4], ALU.add)
            eng.drain()
            eng.tensor_tensor(r8[0], r8[0], r8[1], ALU.mult)
            eng.drain()
            eng.reduce_sum(rcol[:, 3:4], r8[0], axis=AX.X)
            return True
        s_sel = t2p.stage("t2_sel", "vector", t_sel, deps=[(s_exp, lambda n: n + 1)])
        def t_inv(eng, n):
            eng.activation(rcol[:, 2:3], rcol[:, 3:4], AF.Ln)
            eng.drain()
            eng.activation(rcol[:, 2:3], rcol[:, 2:3], AF.Exp, scale=-1.0)
            return True
        s_inv = t2p.stage("t2_inv", "scalar", t_inv, deps=[(s_sel, lambda n: n + 1)])
        def t_rw(eng, n):
            o = rw8[n % 2]
            if n < 2:
                eng.memset(o[:, 8:128], 0.0)
                eng.drain()
            eng.tensor_scalar(o[:, 0:8], r8[0], rcol[:, 2:3], None, ALU.mult)
            return True
        s_rw = t2p.stage("t2_rw", "vector", t_rw, deps=[(s_inv, lambda n: n + 1)])
        def t_st(eng, n):
            return [eng.dma_start(out=rw_s[n * 128:(n + 1) * 128, :], in_=rw8[n % 2])]
        s_st = t2p.stage("t2_st", "gpsimd", t_st, dma=True, deps=[(s_rw, lambda n: n + 1)])
        s_rw["deps"].append((s_st, lambda n: n - 1))
        s_pre["deps"].append((s_rw, lambda n: n))
        s_exp["deps"].append((s_rw, lambda n: n))
        s_sel["deps"].append((s_rw, lambda n: n))
        t2p.run()

        t3p = Pipe(K, blk, 1)
        t3p.stage("t2_rt", "sync",
                  lambda eng, i: [eng.dma_start_transpose(rwT, rw_s)], dma=True)
        t3p.run()

    proj_pass("q", H, lambda oc: hbm3(wq_h, oc), "vector", rope_evac, mk_store(qd_s))
    proj_pass("g", H, lambda oc: hbm3(wg_h, oc), "scalar", cp_evac, mk_store(g_s))

    if upto < 6:
        return nc

    with nc.Block() as blk:
        pp = Pipe(K, blk, 1)
        st_bf = pp.stage("p3_bf", "gpsimd",
                         lambda eng, i: [eng.dma_start(out=bflat_sb, in_=bflat_h)], dma=True)
        def rwx_mm(eng, i):
            for s in range(NS):
                eng.matmul(PS[s][:, 0:S_W], esel, rwT[0:E0, s * S_W:(s + 1) * S_W],
                           start=True, stop=True)
            return True
        st_rm = pp.stage("p3_rwxm", "tensor", rwx_mm)
        def rwx_cp(eng, i):
            for s in range(NS):
                eng.copy(rwx[:, s * S_W:(s + 1) * S_W], PS[s][:, 0:S_W])
            return True
        st_rc = pp.stage("p3_rwxc", "scalar", rwx_cp, deps=[(st_rm, lambda i: 1)])
        pp.run()

    def hw_evac(eng, i, P):
        s = i % NS
        eng.tensor_tensor(hwT[:, s * S_W:(s + 1) * S_W], P[:, 0:S_W],
                          rwx[:, s * S_W:(s + 1) * S_W], ALU.mult)
        return True
    proj_pass("hw", 1, lambda oc: aflat_h, "vector", hw_evac, None)

    def f0_extra_mm(eng, i, P):
        oc, s = divmod(i, NS)
        eng.matmul(P[:, 0:S_W], bflat_sb[:, oc * 128:(oc + 1) * 128],
                   hwT[:, s * S_W:(s + 1) * S_W], start=False, stop=True)
    def f0_xld(eng, i):
        oc, s = divmod(i, NS)
        return [eng.dma_start_transpose(
                    xhiT_t[i % 2],
                    bass.AP(xhi_s.tensor, (s * S_W) * HID + oc * 128, [[HID, S_W], [1, 128]])),
                eng.dma_start_transpose(
                    xloT_t[i % 2],
                    bass.AP(xlo_s.tensor, (s * S_W) * HID + oc * 128, [[HID, S_W], [1, 128]]))]
    with nc.Block() as blk:
        NI = FCN * NS
        pp = Pipe(K, blk, NI + 1, nmax=NI)
        def w_ld(eng, i):
            if i >= NI or i % NS != 0:
                return None
            oc = i // NS
            return [eng.dma_start(out=wbf[oc % 2], in_=hbm3(wbase_h, oc))]
        st_w = pp.stage("f0_w", "gpsimd", w_ld, dma=True, n=FCN)
        st_x = pp.stage("f0_x", "sync", lambda eng, i: None if i >= NI else f0_xld(eng, i),
                        dma=True, dpi=2)
        def mm_emit(eng, i):
            if i >= NI:
                return None
            oc, s = divmod(i, NS)
            P = PS[i % NB]
            for fc in range(FCN):
                eng.matmul(P[:, 0:S_W], wbf[oc % 2][:, fc * 128:(fc + 1) * 128],
                           xn_T[:, fc * T_C + s * S_W: fc * T_C + (s + 1) * S_W],
                           start=(fc == 0), stop=False)
            f0_extra_mm(eng, i, P)
            return True
        st_mm = pp.stage("f0_mm", "tensor", mm_emit, deps=[(st_w, lambda i: i // NS + 1)])
        def ev_a(eng, i):
            if i >= NI:
                return None
            oc, s = divmod(i, NS)
            eng.activation(f0t[i % 2], PS[i % NB][:, 0:S_W], AF.Identity, bias=bbase[:, oc:oc + 1])
            return True
        st_a = pp.stage("f0_a", "scalar", ev_a,
                        deps=[(st_mm, lambda i: i + 1), (st_x, lambda i: i + 1)])
        def ev_v(eng, i):
            if i >= NI:
                return None
            eng.tensor_tensor(f0t[i % 2], f0t[i % 2], xhiT_t[i % 2], ALU.add)
            return True
        st_v = pp.stage("f0_v", "vector", ev_v, deps=[(st_a, lambda i: i + 1)])
        def ev_p(eng, i):
            if i >= NI:
                return None
            eng.tensor_tensor(f0t[i % 2], f0t[i % 2], xloT_t[i % 2], ALU.add)
            return True
        st_p = pp.stage("f0_p", "gpsimd", ev_p, deps=[(st_v, lambda i: i + 1)])
        st_mm["deps"].append((st_a, lambda i: i - NB + 1))
        st_w["deps"].append((st_mm, lambda i: (i // NS - 1) * NS))
        st_x["deps"].append((st_v, lambda i: i - 1))
        st_x["deps"].append((st_p, lambda i: i - 1))
        def st_emit(eng, i):
            if i == 0:
                return None
            j = i - 1
            oc, s = divmod(j, NS)
            return [eng.dma_start(out=f0x_s[oc * 128:(oc + 1) * 128, s * S_W:(s + 1) * S_W],
                                  in_=f0t[j % 2])]
        st_st = pp.stage("f0_st", "gpsimd", st_emit, deps=[(st_p, lambda i: i)], dma=True)
        st_a["deps"].append((st_st, lambda i: i - 1))
        pp.run()

    if upto < 7:
        return nc

    with nc.Block() as blk:
        pp = Pipe(K, blk, 1)
        def d_ld(eng, i):
            eng.wait_ge(cc_ctr.sem, 1)
            return [eng.dma_start(out=pmrow,
                                  in_=bass.AP(ag_out.ap().tensor, 4096 * 128,
                                              [[0, 1], [AGROW * 128, 4], [1, 32]]))]
        st_l = pp.stage("d_ld", "sync", d_ld, dma=True)
        def d_bc(eng, i):
            for q in range(4):
                eng.matmul(PS[0][0:4, q * 32:(q + 1) * 32], ones1f[0:1, 0:4],
                           pmrow[0:1, q * 32:(q + 1) * 32], start=True, stop=True)
            return True
        st_b = pp.stage("d_bc", "tensor", d_bc, deps=[(st_l, lambda i: 1)])
        def d_w(eng, i):
            eng.memset(cfac, 1.0)
            eng.drain()
            for q in range(4):
                eng.tensor_scalar(t2[0:4, 0:32], PS[0][0:4, q * 32:(q + 1) * 32],
                                  emaskc[:, q:q + 1], iemaskc[:, q:q + 1], ALU.mult, ALU.add)
                eng.drain()
                eng.tensor_tensor(cfac, cfac, t2[0:4, 0:32], ALU.mult)
                eng.drain()
            eng.tensor_scalar(w4, cfac, fmaskc[:, 0:1], None, ALU.mult)
            return True
        st_w = pp.stage("d_w", "vector", d_w, deps=[(st_b, lambda i: 1)])
        st_ws = pp.stage("d_ws", "gpsimd",
                         lambda eng, i: [eng.dma_start(out=wb4_s, in_=w4)],
                         dma=True, deps=[(st_w, lambda i: 1)])
        st_wl = pp.stage("d_wl", "sync",
                         lambda eng, i: [eng.dma_start(
                             out=wrow, in_=bass.AP(wb4_s.tensor, 0, [[0, 1], [1, 128]]))],
                         dma=True, deps=[(st_ws, lambda i: 1)])
        def d_wbc(eng, i):
            eng.matmul(PS[1][:, 0:128], ones1f, wrow, start=True, stop=True)
            return True
        st_wb = pp.stage("d_wbc", "tensor", d_wbc, deps=[(st_wl, lambda i: 1)])
        def d_wcp(eng, i):
            eng.copy(wbc_sb, PS[1][:, 0:128])
            return True
        st_wc = pp.stage("d_wcp", "scalar", d_wcp, deps=[(st_wb, lambda i: 1)])
        pp.run()

        sp = Pipe(K, blk, H)
        def e_ld(eng, h):
            return [eng.dma_start(out=spt[(h % 2) * 4 + p],
                                  in_=ag_out.ap()[p * AGROW + h * 128: p * AGROW + (h + 1) * 128, :])
                    for p in range(4)]
        se_l = sp.stage("d2_ld", "sync", e_ld, dma=True, dpi=4)
        def e_acc(eng, h):
            hs = slice(h * 128, (h + 1) * 128)
            eng.tensor_scalar(state[:, hs], spt[(h % 2) * 4], wbc_sb[:, h:h + 1], None, ALU.mult)
            for p in range(1, 4):
                eng.tensor_scalar(t2[:, 0:128], spt[(h % 2) * 4 + p],
                                  wbc_sb[:, p * 32 + h:p * 32 + h + 1], None, ALU.mult)
                eng.drain()
                eng.tensor_tensor(state[:, hs], state[:, hs], t2[:, 0:128], ALU.add)
                eng.drain()
            return True
        se_a = sp.stage("d2_acc", "vector", e_acc, deps=[(se_l, lambda h: h + 1)])
        se_l["deps"].append((se_a, lambda h: h - 1))
        sp.run()

    if upto < 8:
        return nc

    with nc.Block() as blk:
        NIT = KVH * NCH
        pp = Pipe(K, blk, NIT + 1, nmax=NIT)
        def hof(i, h4):
            return (i % 2) * 4 + h4
        def f_ld(eng, i):
            if i >= NIT:
                return None
            kv, n = divmod(i, NCH)
            r = [eng.dma_start(out=kdt5[i % 2],
                               in_=bass.AP(kd_s.tensor, (kv * 128) * T_C + n * 128,
                                           [[T_C, 128], [1, 128]]))]
            for h4 in range(GQ):
                h = kv * GQ + h4
                r.append(eng.dma_start(out=qdt[hof(i, h4)],
                                       in_=bass.AP(qd_s.tensor, (h * 128) * T_C + n * 128,
                                                   [[T_C, 128], [1, 128]])))
                r.append(eng.dma_start(out=xct5[hof(i, h4)],
                                       in_=bass.AP(xc_s.tensor, (h * T_C + n * 128) * D,
                                                   [[D, 128], [1, D]])))
                r.append(eng.dma_start(out=cst5[hof(i, h4)],
                                       in_=bass.AP(cs_s.tensor, (h * NCH + n) * D * D,
                                                   [[D, 128], [1, D]])))
            return r
        st_ld = pp.stage("f_ld", "sync", f_ld, dma=True, dpi=1 + 3 * GQ)
        def f_gr(eng, i):
            if i >= NIT:
                return None
            kv, n = divmod(i, NCH)
            return [eng.dma_start(out=grow5[i % 2],
                                  in_=bass.AP(gT_s.tensor, (kv * GQ) * T_C + n * 128,
                                              [[0, 1], [T_C, GQ], [1, 128]]))]
        st_gr = pp.stage("f_gr", "sync", f_gr, dma=True)
        def f_gt(eng, i):
            if i >= NIT:
                return None
            kv, n = divmod(i, NCH)
            r = []
            for h4 in range(GQ):
                h = kv * GQ + h4
                r.append(eng.dma_start_transpose(
                    gt5[hof(i, h4)],
                    bass.AP(g_s.tensor, (h * 128) * T_C + n * 128, [[T_C, 128], [1, 128]])))
            return r
        st_gt = pp.stage("f_gt", "sync", f_gt, dma=True, dpi=GQ)
        def f_pe1(eng, i):
            if i >= NIT:
                return None
            b0, b1 = PS[(i % 2) * 4], PS[(i % 2) * 4 + 1]
            for h4 in range(GQ):
                eng.matmul(b0[:, h4 * 128:(h4 + 1) * 128], kdt5[i % 2], qdt[hof(i, h4)],
                           start=True, stop=True)
                eng.matmul(b1[:, h4 * 128:(h4 + 1) * 128], ones1f,
                           grow5[i % 2][0:1, h4 * 128:(h4 + 1) * 128], start=True, stop=True)
            return True
        st_p1 = pp.stage("f_pe1", "tensor", f_pe1,
                         deps=[(st_ld, lambda i: i + 1), (st_gr, lambda i: i + 1)])
        def f_m(eng, i):
            if i >= NIT:
                return None
            kv, n = divmod(i, NCH)
            b1 = PS[(i % 2) * 4 + 1]
            for h4 in range(GQ):
                c = n * 32 + kv * GQ + h4
                eng.tensor_scalar(mm5[hof(i, h4)], b1[:, h4 * 128:(h4 + 1) * 128],
                                  G_tok[:, c:c + 1], 0.0, ALU.subtract, ALU.min)
            return True
        st_m = pp.stage("f_m", "vector", f_m, deps=[(st_p1, lambda i: i + 1)])
        def f_dk(eng, i):
            if i >= NIT:
                return None
            kv, n = divmod(i, NCH)
            for h4 in range(GQ):
                h = kv * GQ + h4
                eng.activation(dcy5[hof(i, h4)], mm5[hof(i, h4)], AF.Exp)
                eng.copy(sbf5[hof(i, h4)], state[:, h * 128:(h + 1) * 128])
            return True
        st_dk = pp.stage("f_dk", "scalar", f_dk, deps=[(st_m, lambda i: i + 1)])
        def f_sd(eng, i):
            if i >= NIT:
                return None
            b0 = PS[(i % 2) * 4]
            for h4 in range(GQ):
                eng.tensor_tensor(mm5[hof(i, h4)], b0[:, h4 * 128:(h4 + 1) * 128],
                                  dcy5[hof(i, h4)], ALU.mult)
            return True
        st_sd = pp.stage("f_sd", "vector", f_sd, deps=[(st_dk, lambda i: i + 1)])
        def f_up(eng, i):
            if i == 0:
                return None
            j = i - 1
            kv, n = divmod(j, NCH)
            for h4 in range(GQ):
                h = kv * GQ + h4
                c = n * 32 + h
                hs = slice(h * 128, (h + 1) * 128)
                eng.tensor_scalar(state[:, hs], state[:, hs], dec_bc[:, c:c + 1], None, ALU.mult)
                eng.drain()
                eng.tensor_tensor(state[:, hs], state[:, hs], cst5[hof(j, h4)], ALU.add)
                eng.drain()
            return True
        st_up = pp.stage("f_up", "gpsimd", f_up, deps=[(st_dk, lambda i: i)])
        st_dk["deps"].append((st_up, lambda i: i))
        def f_sdb(eng, i):
            if i >= NIT:
                return None
            for h4 in range(GQ):
                eng.tensor_tensor(sd5[hof(i, h4)], mm5[hof(i, h4)], utri, ALU.mult)
            return True
        st_sb = pp.stage("f_sdb", "gpsimd", f_sdb, deps=[(st_sd, lambda i: i + 1)])
        def f_pe2(eng, i):
            if i == 0:
                return None
            j = i - 1
            b2, b3 = PS[(j % 2) * 4 + 2], PS[(j % 2) * 4 + 3]
            for h4 in range(GQ):
                eng.matmul(b2[:, h4 * 128:(h4 + 1) * 128], sd5[hof(j, h4)], xct5[hof(j, h4)],
                           start=True, stop=True)
                eng.matmul(b3[:, h4 * 128:(h4 + 1) * 128], qdt[hof(j, h4)], sbf5[hof(j, h4)],
                           start=True, stop=True)
            return True
        st_p2 = pp.stage("f_pe2", "tensor", f_pe2,
                         deps=[(st_sb, lambda i: i), (st_dk, lambda i: i)])
        def f_yc(eng, i):
            if i == 0:
                return None
            j = i - 1
            kv, n = divmod(j, NCH)
            b2 = PS[(j % 2) * 4 + 2]
            b3 = PS[(j % 2) * 4 + 3]
            for h4 in range(GQ):
                c = n * 32 + kv * GQ + h4
                eng.tensor_scalar(yt5[hof(j, h4)], b3[:, h4 * 128:(h4 + 1) * 128],
                                  eG_tok[:, c:c + 1], None, ALU.mult)
                eng.drain()
                eng.tensor_tensor(yt5[hof(j, h4)], yt5[hof(j, h4)],
                                  b2[:, h4 * 128:(h4 + 1) * 128], ALU.add)
                eng.drain()
            return True
        st_yc = pp.stage("f_yc", "vector", f_yc, deps=[(st_p2, lambda i: i)])
        st_yb = st_yc
        def f_rms(eng, i):
            if i == 0:
                return None
            j = i - 1
            for h4 in range(GQ):
                k = hof(j, h4)
                eng.activation(sq5, yt5[k], AF.Square, accum_out=ivr5[k])
                eng.drain()
                eng.activation(ivr5[k], ivr5[k], AF.Ln, bias=epsc[:, 0:1], scale=1.0 / 128.0)
                eng.drain()
                eng.activation(ivr5[k], ivr5[k], AF.Exp, scale=-0.5)
                eng.activation(gsil5[k], gt5[k], AF.Silu)
                eng.drain()
            return True
        st_rm = pp.stage("f_rms", "scalar", f_rms,
                         deps=[(st_yb, lambda i: i), (st_gt, lambda i: i)])
        def f_y2(eng, i):
            if i == 0:
                return None
            j = i - 1
            for h4 in range(GQ):
                k = hof(j, h4)
                eng.tensor_scalar(yt5[k], yt5[k], ivr5[k][:, 0:1], None, ALU.mult)
            return True
        st_y2 = pp.stage("f_y2", "vector", f_y2, deps=[(st_rm, lambda i: i)])
        def f_y2b(eng, i):
            if i == 0:
                return None
            j = i - 1
            for h4 in range(GQ):
                k = hof(j, h4)
                eng.tensor_tensor(yt5[k], yt5[k], gnorm_bc, ALU.mult)
            return True
        st_yg2 = pp.stage("f_y2b", "gpsimd", f_y2b, deps=[(st_y2, lambda i: i)])
        def f_yg(eng, i):
            if i == 0:
                return None
            j = i - 1
            for h4 in range(GQ):
                k = hof(j, h4)
                eng.tensor_tensor(ygs5[k], yt5[k], gsil5[k], ALU.mult)
            return True
        st_yg = pp.stage("f_yg", "vector", f_yg, deps=[(st_yg2, lambda i: i)])
        def f_st(eng, i):
            if i == 0:
                return None
            j = i - 1
            kv, n = divmod(j, NCH)
            r = []
            for h4 in range(GQ):
                h = kv * GQ + h4
                r.append(eng.dma_start(
                    out=yg_s[n * 128:(n + 1) * 128, h * 128:(h + 1) * 128],
                    in_=ygs5[hof(j, h4)]))
            return r
        st_fs = pp.stage("f_st", "gpsimd", f_st, dma=True, dpi=GQ,
                         deps=[(st_yg, lambda i: i)])
        st_p1["deps"].append((st_sd, lambda i: i - 1))
        st_p1["deps"].append((st_m, lambda i: i - 1))
        st_p2["deps"].append((st_yc, lambda i: i - 1))
        st_p2["deps"].append((st_yb, lambda i: i - 1))
        st_ld["deps"].append((st_p2, lambda i: i - 1))
        st_ld["deps"].append((st_up, lambda i: i - 1))
        st_gr["deps"].append((st_p1, lambda i: i - 1))
        st_gt["deps"].append((st_rm, lambda i: i - 1))
        st_m["deps"].append((st_sd, lambda i: i - 1))
        st_dk["deps"].append((st_sd, lambda i: i - 1))
        st_dk["deps"].append((st_p2, lambda i: i - 1))
        st_sb["deps"].append((st_p2, lambda i: i - 1))
        st_yc["deps"].append((st_yg, lambda i: i - 1))
        st_rm["deps"].append((st_y2, lambda i: i - 1))
        st_yg["deps"].append((st_fs, lambda i: i - 1))
        st_rm["deps"].append((st_yg, lambda i: i - 1))
        pp.run()

    if upto < 9:
        return nc

    with nc.Block() as blk:
        pp = Pipe(K, blk, 1)
        pp.stage("p4_yt", "sync",
                 lambda eng, i: [eng.dma_start_transpose(ygT[:, fc * T_C:(fc + 1) * T_C],
                                                         yg_s[:, fc * 128:(fc + 1) * 128])
                                 for fc in range(FCN)],
                 dma=True, dpi=FCN)
        pp.run()

    with nc.Block() as blk:
        NI = FCN * NS
        pp = Pipe(K, blk, NI + 1, nmax=NI)
        def w_ld(eng, i):
            if i >= NI or i % NS != 0:
                return None
            oc = i // NS
            return [eng.dma_start(out=wbf[oc % 2], in_=hbm3(wo_h, oc))]
        st_w = pp.stage("p4_w", "gpsimd", w_ld, dma=True, n=FCN)
        def f_ld(eng, i):
            if i >= NI:
                return None
            oc, s = divmod(i, NS)
            return [eng.dma_start(out=f0t[i % 2],
                                  in_=f0x_s[oc * 128:(oc + 1) * 128, s * S_W:(s + 1) * S_W])]
        st_f = pp.stage("p4_f", "sync", f_ld, dma=True)
        def mm_emit(eng, i):
            if i >= NI:
                return None
            oc, s = divmod(i, NS)
            P = PS[i % NB]
            for fc in range(FCN):
                eng.matmul(P[:, 0:S_W], wbf[oc % 2][:, fc * 128:(fc + 1) * 128],
                           ygT[:, fc * T_C + s * S_W: fc * T_C + (s + 1) * S_W],
                           start=(fc == 0), stop=(fc == FCN - 1))
            return True
        st_mm = pp.stage("p4_mm", "tensor", mm_emit, deps=[(st_w, lambda i: i // NS + 1)])
        def ev_v(eng, i):
            if i >= NI:
                return None
            eng.tensor_tensor(x1t[i % 2], PS[i % NB][:, 0:S_W], f0t[i % 2], ALU.add)
            return True
        st_v = pp.stage("p4_v", "vector", ev_v,
                        deps=[(st_mm, lambda i: i + 1), (st_f, lambda i: i + 1)])
        def ev_h(eng, i):
            if i >= NI:
                return None
            eng.activation(x1hi_t[i % 2], x1t[i % 2], AF.Copy)
            return True
        st_h = pp.stage("p4_h", "scalar", ev_h, deps=[(st_v, lambda i: i + 1)])
        def ev_l(eng, i):
            if i >= NI:
                return None
            eng.tensor_tensor(x1lo_t[i % 2], x1t[i % 2], x1hi_t[i % 2], ALU.subtract)
            return True
        st_l = pp.stage("p4_l", "gpsimd", ev_l, deps=[(st_h, lambda i: i + 1)])
        st_mm["deps"].append((st_v, lambda i: i - NB + 1))
        st_w["deps"].append((st_mm, lambda i: (i // NS - 1) * NS))
        st_f["deps"].append((st_v, lambda i: i - 1))
        st_v["deps"].append((st_l, lambda i: i - 1))
        def st_emit(eng, i):
            if i == 0:
                return None
            j = i - 1
            oc, s = divmod(j, NS)
            sl = slice(s * S_W, (s + 1) * S_W)
            return [eng.dma_start(out=x1hi_s[oc * 128:(oc + 1) * 128, sl], in_=x1hi_t[j % 2]),
                    eng.dma_start(out=x1lo_s[oc * 128:(oc + 1) * 128, sl], in_=x1lo_t[j % 2])]
        st_st = pp.stage("p4_st", "gpsimd", st_emit, deps=[(st_l, lambda i: i)],
                         dma=True, dpi=2)
        st_h["deps"].append((st_st, lambda i: i - 1))
        st_l["deps"].append((st_st, lambda i: i - 1))
        pp.run()

    if upto < 10:
        return nc

    with nc.Block() as blk:
        pp = Pipe(K, blk, NP1)
        def t_ld(eng, i):
            c = slice(i * 128, (i + 1) * 128)
            return [eng.dma_start_transpose(p5hi[i % 2],
                        bass.AP(x1hi_s.tensor, i * 128, [[T_C, HID], [1, 128]])),
                    eng.dma_start_transpose(p5lo[i % 2],
                        bass.AP(x1lo_s.tensor, i * 128, [[T_C, HID], [1, 128]]))]
        st_ld = pp.stage("p5_ld", "sync", t_ld, dma=True, dpi=2)
        def p5_add(eng, i):
            eng.tensor_tensor(p5x[i % 2], p5hi[i % 2], p5lo[i % 2], ALU.add)
            return True
        st_ad = pp.stage("p5_add", "vector", p5_add, deps=[(st_ld, lambda i: i + 1)])
        def p5_sq(eng, i):
            eng.activation(p5sq, p5x[i % 2], AF.Square, accum_out=t2[:, 0:1])
            return True
        st_sq = pp.stage("p5_sq", "scalar", p5_sq, deps=[(st_ad, lambda i: i + 1)])
        def p5_stats(eng, i):
            eng.reduce_sum(t2[:, 1:2], p5x[i % 2], axis=AX.X)
            eng.drain()
            eng.tensor_scalar(t2[:, 1:2], t2[:, 1:2], 1.0 / HID, None, ALU.mult)
            eng.tensor_scalar(t2[:, 0:1], t2[:, 0:1], 1.0 / HID, None, ALU.mult)
            eng.drain()
            eng.scalar_tensor_tensor(t2[:, 2:3], t2[:, 1:2], 1.0, t2[:, 1:2], ALU.mult, ALU.mult)
            eng.drain()
            eng.tensor_tensor(t2[:, 0:1], t2[:, 0:1], t2[:, 2:3], ALU.subtract)
            return True
        st_v1 = pp.stage("p5_v1", "vector", p5_stats, deps=[(st_sq, lambda i: i + 1)])
        def p5_r(eng, i):
            eng.activation(t2[:, 3:4], t2[:, 0:1], AF.Ln, bias=epsc[:, 0:1])
            eng.drain()
            eng.activation(t2[:, 3:4], t2[:, 3:4], AF.Exp, scale=-0.5)
            return True
        st_r = pp.stage("p5_r", "scalar", p5_r, deps=[(st_v1, lambda i: i + 1)])
        def p5_apply(eng, i):
            eng.tensor_scalar(p5o[i % 2], p5x[i % 2], t2[:, 1:2], t2[:, 3:4],
                              ALU.subtract, ALU.mult)
            return True
        st_ap = pp.stage("p5_ap", "vector", p5_apply, deps=[(st_r, lambda i: i + 1)])
        def p5_st(eng, i):
            return [eng.dma_start(out=xn1_s[i * 128:(i + 1) * 128, :], in_=p5o[i % 2])]
        st_st = pp.stage("p5_st", "gpsimd", p5_st, deps=[(st_ap, lambda i: i + 1)], dma=True)
        st_ld["deps"].append((st_ap, lambda i: i - 1))
        st_ad["deps"].append((st_ap, lambda i: i))
        st_ap["deps"].append((st_st, lambda i: i - 1))
        st_sq["deps"].append((st_r, lambda i: i))
        st_v1["deps"].append((st_ap, lambda i: i))
        st_v1["deps"].append((st_r, lambda i: i))
        pp.run()

    if upto < 11:
        return nc

    with nc.Block() as blk:
        pp = Pipe(K, blk, FCN + 1, nmax=FCN)
        def xt_ld(eng, i):
            if i > 0:
                return None
            return [eng.dma_start_transpose(xn1T[:, fc * T_C:(fc + 1) * T_C],
                                            xn1_s[:, fc * 128:(fc + 1) * 128])
                    for fc in range(FCN)]
        pp.stage("p6_xt", "sync", xt_ld, dma=True, dpi=FCN, nsem=1, n=1)
        def m_ld(eng, i):
            if i >= FCN:
                return None
            return [eng.dma_start(out=mbf[i % 2], in_=maskT_h[i * 128:(i + 1) * 128, :])]
        st_ml = pp.stage("p6_ml", "gpsimd", m_ld, dma=True)
        def m_mm(eng, i):
            if i >= FCN:
                return None
            for s in range(8):
                eng.matmul(PS[s][0:4, 0:512], b32, mbf[i % 2][:, s * 512:(s + 1) * 512],
                           start=True, stop=True)
            return True
        st_mm = pp.stage("p6_mm", "tensor", m_mm, deps=[(st_ml, lambda i: i + 1)])
        def m_cp(eng, i):
            if i >= FCN:
                return None
            for s in range(8):
                eng.copy(s1sb[:, s * 512:(s + 1) * 512], PS[s][0:4, 0:512])
            return True
        st_cp = pp.stage("p6_cp", "scalar", m_cp, deps=[(st_mm, lambda i: i + 1)])
        def m_red(eng, i):
            if i >= FCN:
                return None
            eng.tensor_tensor(mr1, bass.AP(s1sb.tensor, 0, [[HID, 4], [2, 2048]]),
                              bass.AP(s1sb.tensor, 1, [[HID, 4], [2, 2048]]), ALU.add)
            eng.drain()
            eng.tensor_tensor(mr2, bass.AP(mr1.tensor, 0, [[2048, 4], [2, 1024]]),
                              bass.AP(mr1.tensor, 1, [[2048, 4], [2, 1024]]), ALU.add)
            eng.drain()
            eng.tensor_tensor(mr3, bass.AP(mr2.tensor, 0, [[1024, 4], [2, 512]]),
                              bass.AP(mr2.tensor, 1, [[1024, 4], [2, 512]]), ALU.add)
            eng.drain()
            eng.tensor_tensor(mr4, bass.AP(mr3.tensor, 0, [[512, 4], [2, 256]]),
                              bass.AP(mr3.tensor, 1, [[512, 4], [2, 256]]), ALU.add)
            eng.drain()
            eng.tensor_tensor(mr5[i % 2], bass.AP(mr4.tensor, 0, [[256, 4], [2, 128]]),
                              bass.AP(mr4.tensor, 1, [[256, 4], [2, 128]]), ALU.add)
            return True
        st_rd = pp.stage("p6_rd", "vector", m_red, deps=[(st_cp, lambda i: i + 1)])
        st_mm["deps"].append((st_rd, lambda i: i - 1))
        st_mm["deps"].append((st_cp, lambda i: i))
        st_cp["deps"].append((st_rd, lambda i: i))
        def m_st(eng, i):
            if i == 0:
                return None
            j = i - 1
            return [eng.dma_start(out=bmb_s[j * 4:(j + 1) * 4, :], in_=mr5[j % 2])]
        st_bs = pp.stage("p6_bs", "gpsimd", m_st, dma=True, deps=[(st_rd, lambda i: i)])
        st_rd["deps"].append((st_bs, lambda i: i - 1))
        pp.run()

        bp = Pipe(K, blk, 1)
        sb_l = bp.stage("p6_bl", "sync",
                        lambda eng, i: [eng.dma_start(out=bmld, in_=bmb_s)], dma=True)
        def b_ge(eng, i):
            eng.tensor_scalar(bm01t, bmld, 0.5, None, ALU.is_ge)
            return True
        sb_g = bp.stage("p6_bg", "vector", b_ge, deps=[(sb_l, lambda i: 1)])
        bp.stage("p6_b01", "gpsimd",
                 lambda eng, i: [eng.dma_start(out=bm01_s, in_=bm01t)],
                 dma=True, deps=[(sb_g, lambda i: 1)])
        bp.run()

        mp = Pipe(K, blk, FCN)
        sm_l = mp.stage("p6_el", "sync",
                        lambda eng, fc: [eng.dma_start(out=bmr[fc % 2],
                                                       in_=bm01_s[fc * 4:(fc + 1) * 4, :])],
                        dma=True)
        def e_mm(eng, fc):
            eng.matmul(PS[fc % 2][:, 0:128], sel4, bmr[fc % 2], start=True, stop=True)
            return True
        sm_m = mp.stage("p6_em", "tensor", e_mm, deps=[(sm_l, lambda fc: fc + 1)])
        def e_cp(eng, fc):
            eng.copy(mexp[:, fc * 128:(fc + 1) * 128], PS[fc % 2][:, 0:128])
            return True
        sm_c = mp.stage("p6_ec", "scalar", e_cp, deps=[(sm_m, lambda fc: fc + 1)])
        sm_m["deps"].append((sm_c, lambda fc: fc - 1))
        sm_l["deps"].append((sm_m, lambda fc: fc - 1))
        mp.run()

    if upto < 12:
        return nc

    with nc.Block() as blk:
        pp = Pipe(K, blk, 1)
        def r_ld(eng, i):
            return [eng.dma_start(out=rw1bf, in_=rw1_h),
                    eng.dma_start(out=expb_sb, in_=expb_h)]
        st_rl = pp.stage("p7_rl", "gpsimd", r_ld, dma=True, dpi=2)
        def r_mm(eng, i):
            for s in range(NS):
                for fc in range(FCN):
                    eng.matmul(PS[s][0:E1, 0:S_W], rw1bf[:, fc * E1:(fc + 1) * E1],
                               xn1T[:, fc * T_C + s * S_W: fc * T_C + (s + 1) * S_W],
                               start=(fc == 0), stop=(fc == FCN - 1))
            return True
        st_rm = pp.stage("p7_rm", "tensor", r_mm, deps=[(st_rl, lambda i: 1)])
        def r_cp(eng, i):
            for s in range(NS):
                eng.activation(l1T[:, s * S_W:(s + 1) * S_W], PS[s][0:E1, 0:S_W],
                               AF.Identity, bias=rb1[:, 0:1])
            return True
        st_rc = pp.stage("p7_rc", "scalar", r_cp, deps=[(st_rm, lambda i: 1)])
        pp.run()

        tp = Pipe(K, blk, NCH)
        def q_tp(eng, n):
            eng.transpose(PS[4 + n % 2][:, 0:E1], l1T[:, n * 128:(n + 1) * 128], id64)
            return True
        sq_t = tp.stage("p7_tp", "tensor", q_tp)
        def q_cp(eng, n):
            eng.tensor_copy(l1tok, PS[4 + n % 2][:, 0:E1])
            eng.drain()
            eng.reduce_max(r64[:, 0:1], l1tok, axis=AX.X)
            eng.drain()
            eng.tensor_scalar(r64[:, 1:2], r64[:, 0:1], -1.0, None, ALU.mult)
            return True
        sq_c = tp.stage("p7_cp", "vector", q_cp, deps=[(sq_t, lambda n: n + 1)])
        sq_t["deps"].append((sq_c, lambda n: n - 1))
        def q_e(eng, n):
            eng.activation(etok, l1tok, AF.Exp, bias=r64[:, 1:2], accum_out=r64[:, 2:3])
            return True
        sq_e = tp.stage("p7_e", "scalar", q_e, deps=[(sq_c, lambda n: n + 1)])
        def q_sel(eng, n):
            eng.tensor_copy(ew, etok)
            eng.memset(sel8, 0.0)
            eng.drain()
            for k in range(K1):
                eng.reduce_max(r64[:, 3:4], ew, axis=AX.X)
                eng.drain()
                eng.tensor_scalar(cc1, ew, r64[:, 3:4], None, ALU.is_ge)
                eng.drain()
                eng.tensor_tensor_scan(wdtok, cc1, zer64, 0.0, ALU.add, ALU.add)
                eng.drain()
                eng.tensor_scalar(wdtok, wdtok, 1.0, None, ALU.is_equal)
                eng.drain()
                eng.tensor_tensor(cc1, cc1, wdtok, ALU.mult)
                eng.drain()
                eng.tensor_tensor(sel8, sel8, cc1, ALU.add)
                eng.drain()
                eng.scalar_tensor_tensor(ew, cc1, -2.0, ew, ALU.mult, ALU.add)
                eng.drain()
            eng.tensor_tensor(wdtok, etok, sel8, ALU.mult)
            return True
        sq_s = tp.stage("p7_sel", "vector", q_sel, deps=[(sq_e, lambda n: n + 1)])
        def q_inv(eng, n):
            eng.activation(r64[:, 4:5], r64[:, 2:3], AF.Ln)
            eng.drain()
            eng.activation(r64[:, 4:5], r64[:, 4:5], AF.Exp, scale=-1.0)
            return True
        sq_i = tp.stage("p7_inv", "scalar", q_inv, deps=[(sq_e, lambda n: n + 1)])
        def q_wd(eng, n):
            if n < 2:
                eng.memset(wdbf[n % 2][:, E1:128], 0.0)
            eng.tensor_scalar(wdbf[n % 2][:, 0:E1], wdtok, r64[:, 4:5], None, ALU.mult)
            eng.tensor_scalar(wdtok, wdtok, r64[:, 4:5], None, ALU.mult)
            eng.drain()
            eng.reduce_sum(wsc[n % 2], wdtok, axis=AX.X)
            return True
        sq_w = tp.stage("p7_wd", "vector", q_wd,
                        deps=[(sq_s, lambda n: n + 1), (sq_i, lambda n: n + 1)])
        def q_st(eng, n):
            return [eng.dma_start(out=wd_s[n * 128:(n + 1) * 128, :], in_=wdbf[n % 2]),
                    eng.dma_start(out=bass.AP(wsb_s.tensor, n * 128, [[0, 1], [1, 128]]),
                                  in_=wsc[n % 2])]
        sq_st = tp.stage("p7_st", "gpsimd", q_st, dma=True, dpi=2, deps=[(sq_w, lambda n: n + 1)])
        sq_w["deps"].append((sq_st, lambda n: n - 1))
        sq_c["deps"].append((sq_e, lambda n: n))
        sq_e["deps"].append((sq_s, lambda n: n))
        sq_e["deps"].append((sq_i, lambda n: n))
        sq_s["deps"].append((sq_w, lambda n: n))
        sq_i["deps"].append((sq_w, lambda n: n))
        tp.run()

        wp2 = Pipe(K, blk, NS)
        def ws_ld(eng, s):
            return [eng.dma_start(out=t3[0:1, 0:S_W],
                                  in_=bass.AP(wsb_s.tensor, s * S_W, [[0, 1], [1, S_W]]))]
        w_l = wp2.stage("p7_wl", "sync", ws_ld, dma=True, nsem=1)
        def ws_mm(eng, s):
            eng.matmul(PS[6][:, 0:S_W], ones1f, t3[0:1, 0:S_W], start=True, stop=True)
            return True
        w_m = wp2.stage("p7_wm", "tensor", ws_mm, deps=[(w_l, lambda s: s + 1)])
        def ws_cp(eng, s):
            eng.copy(wsum_bc[:, s * S_W:(s + 1) * S_W], PS[6][:, 0:S_W])
            return True
        w_c = wp2.stage("p7_wc", "scalar", ws_cp, deps=[(w_m, lambda s: s + 1)])
        w_l["deps"].append((w_m, lambda s: s))
        w_m["deps"].append((w_c, lambda s: s))
        wp2.run()

        wt = Pipe(K, blk, 1)
        wt.stage("p7_wt", "sync",
                 lambda eng, i: [eng.dma_start_transpose(wdT, wd_s)], dma=True)
        wt.run()

    if upto < 13:
        return nc

    with nc.Block() as blk:
        NI = FCN * NS
        pp = Pipe(K, blk, NI + 1, nmax=NI)
        def w_ld(eng, i):
            if i >= NI or i % NS != 0:
                return None
            oc = i // NS
            return [eng.dma_start(out=wbf[oc % 2], in_=hbm3(w1_h, oc))]
        st_w = pp.stage("b6_w", "gpsimd", w_ld, dma=True, n=FCN)
        def w_msk(eng, i):
            if i >= NI or i % NS != 0:
                return None
            oc = i // NS
            mAP = bass.AP(mexp.tensor, oc * 4, [[FCN * 128, 128], [128, FCN], [1, 4], [0, 32]])
            eng.tensor_tensor(wbf[oc % 2], wbf[oc % 2], mAP, ALU.mult)
            return True
        st_k = pp.stage("b6_msk", "gpsimd", w_msk, deps=[(st_w, lambda i: i // NS + 1)], n=FCN)
        def mm_emit(eng, i):
            if i >= NI:
                return None
            oc, s = divmod(i, NS)
            P = PS[i % NB]
            for fc in range(FCN):
                eng.matmul(P[:, 0:S_W], wbf[oc % 2][:, fc * 128:(fc + 1) * 128],
                           xn1T[:, fc * T_C + s * S_W: fc * T_C + (s + 1) * S_W],
                           start=(fc == 0), stop=False)
            eng.matmul(P[:, 0:S_W], expb_sb[:, oc * 128:(oc + 1) * 128],
                       wdT[0:E1, s * S_W:(s + 1) * S_W], start=False, stop=True)
            return True
        st_mm = pp.stage("b6_mm", "tensor", mm_emit, deps=[(st_k, lambda i: i // NS + 1)])
        def ev_a(eng, i):
            if i >= NI:
                return None
            oc, s = divmod(i, NS)
            eng.activation(a1t0[i % 2], PS[i % NB][:, 0:S_W], AF.Identity, bias=bias1c[:, oc:oc + 1])
            return True
        st_a = pp.stage("b6_a", "scalar", ev_a, deps=[(st_mm, lambda i: i + 1)])
        def ev_v(eng, i):
            if i >= NI:
                return None
            oc, s = divmod(i, NS)
            eng.tensor_tensor(a1t1, xn1T[:, oc * T_C + s * S_W: oc * T_C + (s + 1) * S_W],
                              wsum_bc[:, s * S_W:(s + 1) * S_W], ALU.mult)
            eng.drain()
            eng.tensor_tensor(a1t0[i % 2], a1t0[i % 2], a1t1, ALU.add)
            return True
        st_v = pp.stage("b6_v", "vector", ev_v, deps=[(st_a, lambda i: i + 1)])
        def ev_h(eng, i):
            if i >= NI:
                return None
            eng.activation(afhi_t[i % 2], a1t0[i % 2], AF.Copy)
            return True
        st_h = pp.stage("b6_h", "scalar", ev_h, deps=[(st_v, lambda i: i + 1)])
        def ev_l(eng, i):
            if i >= NI:
                return None
            eng.tensor_tensor(aflo_t[i % 2], a1t0[i % 2], afhi_t[i % 2], ALU.subtract)
            return True
        st_l = pp.stage("b6_l", "vector", ev_l, deps=[(st_h, lambda i: i + 1)])
        st_mm["deps"].append((st_v, lambda i: i - NB + 1))
        st_w["deps"].append((st_mm, lambda i: (i // NS - 1) * NS))
        st_a["deps"].append((st_v, lambda i: i - 1))
        st_a["deps"].append((st_l, lambda i: i - 1))
        def st_emit(eng, i):
            if i == 0:
                return None
            j = i - 1
            oc, s = divmod(j, NS)
            sl = slice(s * S_W, (s + 1) * S_W)
            return [eng.dma_start(out=afhi_s[oc * 128:(oc + 1) * 128, sl], in_=afhi_t[j % 2]),
                    eng.dma_start(out=aflo_s[oc * 128:(oc + 1) * 128, sl], in_=aflo_t[j % 2])]
        st_st = pp.stage("b6_st", "gpsimd", st_emit,
                         deps=[(st_l, lambda i: i)], dma=True, dpi=2)
        st_h["deps"].append((st_st, lambda i: i - 1))
        st_l["deps"].append((st_st, lambda i: i - 1))
        pp.run()

    with nc.Block() as blk:
        pp = Pipe(K, blk, NP1)
        def t_ld(eng, i):
            return [eng.dma_start_transpose(p8a[(i % 2) * 2],
                        bass.AP(x1hi_s.tensor, i * 128, [[T_C, HID], [1, 128]])),
                    eng.dma_start_transpose(p8a[(i % 2) * 2 + 1],
                        bass.AP(x1lo_s.tensor, i * 128, [[T_C, HID], [1, 128]])),
                    eng.dma_start_transpose(p8b[(i % 2) * 2],
                        bass.AP(afhi_s.tensor, i * 128, [[T_C, HID], [1, 128]])),
                    eng.dma_start_transpose(p8b[(i % 2) * 2 + 1],
                        bass.AP(aflo_s.tensor, i * 128, [[T_C, HID], [1, 128]]))]
        st_ld = pp.stage("p8_ld", "sync", t_ld, dma=True, dpi=4)
        def p8_v(eng, i):
            eng.tensor_tensor(p8o[i % 2], p8a[(i % 2) * 2], p8a[(i % 2) * 2 + 1], ALU.add)
            return True
        st_v = pp.stage("p8_v", "vector", p8_v, deps=[(st_ld, lambda i: i + 1)])
        def p8_p(eng, i):
            eng.tensor_tensor(p8t[i % 2], p8b[(i % 2) * 2], p8b[(i % 2) * 2 + 1], ALU.add)
            return True
        st_p = pp.stage("p8_p", "gpsimd", p8_p, deps=[(st_ld, lambda i: i + 1)])
        def p8_v2(eng, i):
            eng.tensor_tensor(p8o[i % 2], p8o[i % 2], p8t[i % 2], ALU.add)
            return True
        st_v2 = pp.stage("p8_v2", "vector", p8_v2,
                         deps=[(st_p, lambda i: i + 1)])
        def p8_st(eng, i):
            return [eng.dma_start(out=out_h[i * 128:(i + 1) * 128, :], in_=p8o[i % 2])]
        st_st = pp.stage("p8_st", "sync", p8_st, dma=True, deps=[(st_v2, lambda i: i + 1)])
        st_ld["deps"].append((st_v2, lambda i: i - 1))
        st_v["deps"].append((st_st, lambda i: i - 1))
        st_p["deps"].append((st_v2, lambda i: i - 1))
        pp.run()

    with nc.Block() as blk:
        fp = Pipe(K, blk, 1)
        fp.stage("fin", "sync", lambda eng, i: None)
        fp.run()
    return nc


def _tile_weight(w):
    O, F = w.shape
    oc, fcn = O // 128, F // 128
    return np.ascontiguousarray(
        w.reshape(oc, 128, fcn, 128).transpose(0, 3, 2, 1).reshape(oc, 128, F),
        dtype=np.float32)


def prep_in_maps(inputs, T_C=2048):
    x = np.asarray(inputs["L0_attn_hidden_states"], dtype=np.float32)
    B, L, _ = x.shape
    segs_per_b = L // T_C
    wq = _tile_weight(np.asarray(inputs["L0_attn_Wq"], np.float32))
    wk = _tile_weight(np.asarray(inputs["L0_attn_Wk"], np.float32))
    wv = _tile_weight(np.asarray(inputs["L0_attn_Wv"], np.float32))
    wg = _tile_weight(np.asarray(inputs["L0_attn_Wg"], np.float32))
    wo = _tile_weight(np.asarray(inputs["L0_attn_Wo"], np.float32))
    wbase = _tile_weight(np.asarray(inputs["L0_ffn_W_base"], np.float32))
    w1 = _tile_weight(np.asarray(inputs["L1_attn_weight"], np.float32))
    win = np.ascontiguousarray(
        np.asarray(inputs["L0_attn_Win"], np.float32).reshape(32, FCN, 128)
        .transpose(2, 1, 0).reshape(128, FCN * 32))
    wgate = np.ascontiguousarray(
        np.asarray(inputs["L0_ffn_W_gate"], np.float32).reshape(E0, FCN, 128)
        .transpose(2, 1, 0).reshape(128, FCN * E0))
    aflat = np.ascontiguousarray(
        np.asarray(inputs["L0_ffn_A"], np.float32).reshape(E0 * R0, FCN, 128)
        .transpose(2, 1, 0).reshape(128, FCN * E0 * R0))
    bflat = np.ascontiguousarray(
        np.asarray(inputs["L0_ffn_B"], np.float32).transpose(0, 2, 1).reshape(E0 * R0, HID))
    rw1 = np.ascontiguousarray(
        np.asarray(inputs["L1_ffn_router_weight"], np.float32).reshape(E1, FCN, 128)
        .transpose(2, 1, 0).reshape(128, FCN * E1))
    expb = np.ascontiguousarray(np.asarray(inputs["L1_ffn_expert_bias"], np.float32))
    maskT = np.ascontiguousarray(np.asarray(inputs["L1_attn_mask"], np.int32).T)
    bbase = np.ascontiguousarray(
        np.asarray(inputs["L0_ffn_b_base"], np.float32).reshape(32, 128).T)
    bias1 = np.ascontiguousarray(
        np.asarray(inputs["L1_attn_bias"], np.float32).reshape(32, 128).T)
    dtb = np.asarray(inputs["L0_attn_dt_bias"], np.float32).reshape(32, 1)
    alog = np.asarray(inputs["L0_attn_A_log_bias"], np.float32).reshape(32, 1)
    gnorm = np.asarray(inputs["L0_attn_gnorm_w"], np.float32).reshape(1, 128)
    rb1 = np.asarray(inputs["L1_ffn_router_bias"], np.float32).reshape(E1, 1)

    in_maps = []
    for c in range(B * segs_per_b):
        b, s = c // segs_per_b, c % segs_per_b
        fmask = np.zeros((4, 1), np.float32)
        emask = np.zeros((4, 4), np.float32)
        for sp in range(4):
            if sp < s:
                fmask[sp, 0] = 1.0
            for rr in range(4):
                if sp < rr < s:
                    emask[sp, rr] = 1.0
        pos = np.arange(s * T_C, (s + 1) * T_C, dtype=np.float64)
        invf = 1.0 / 10000.0 ** (np.arange(0, 128, 2, dtype=np.float64) / 128.0)
        ang = invf[:, None] * pos[None, :]
        ropetab = np.concatenate([np.cos(ang), np.sin(ang)], 0).astype(np.float32)
        in_maps.append({
            "x": np.ascontiguousarray(x[b, s * T_C:(s + 1) * T_C, :]),
            "ropetab": ropetab,
            "fmask": fmask, "emask": emask, "iemask": 1.0 - emask,
            "wq": wq, "wk": wk, "wv": wv, "wg": wg, "win": win, "wo": wo,
            "wbase": wbase, "wgate": wgate, "aflat": aflat, "bflat": bflat,
            "w1": w1, "maskT": maskT, "rw1": rw1, "expb": expb,
            "bbase": bbase, "bias1": bias1, "dtb": dtb, "alog": alog,
            "gnorm": gnorm, "rb1": rb1,
        })
    return in_maps


def kernel(**inputs):
    T_C = 2048
    in_maps = prep_in_maps(inputs, T_C)
    nc = build_nc(T_C)
    res = run_bass_kernel_spmd(nc, in_maps, core_ids=list(range(NCORES)))
    x = np.asarray(inputs["L0_attn_hidden_states"])
    B, L, _ = x.shape
    segs = L // T_C
    out = np.empty((B, L, HID), dtype=np.float32)
    for c in range(B * segs):
        b, s = c // segs, c % segs
        out[b, s * T_C:(s + 1) * T_C, :] = res.results[c]["out"]
    return out
